# version 08
# baseline (speedup 1.0000x reference)
import contextlib
import numpy as np

import concourse.bass as bass
import concourse.mybir as mybir
from concourse.bass_utils import run_bass_kernel_spmd

F32 = mybir.dt.float32
F16 = mybir.dt.float16
AF = mybir.ActivationFunctionType
ALU = mybir.AluOpType

C = 2048
H = 2816
NFF = 8192
NCC = C // 128
NHH = H // 128
NNB = NFF // 128
WARM = 128
OWN = 2048
T = WARM + OWN
EPS = 1e-5

TCOLS = [(0, 512), (512, 512), (1024, 512), (1536, 512), (2048, 128)]
RCOLS = [(128, 512), (640, 512), (1152, 512), (1664, 512)]
OCOLS2 = [(0, 1024), (1024, 1024)]


def build_program(debug=False, phases="ABCD", bparts="krso"):
    nc = bass.Bass("TRN2", target_bir_lowering=False,
                   detect_race_conditions=False)
    es = contextlib.ExitStack()
    cnt = {}

    def sem(name):
        h = es.enter_context(nc.semaphore(name))
        cnt[name] = 0
        return h

    def sb(name, shape, dtype, **kw):
        return es.enter_context(nc.sbuf_tensor(name, shape, dtype, **kw))

    x_fm = nc.dram_tensor("x_fm", [C, T], F32, kind="ExternalInput").ap()
    sel = nc.dram_tensor("sel", [128, 1], F32, kind="ExternalInput").ap()
    wks = nc.dram_tensor("WkS", [NHH, 128, C], F32, kind="ExternalInput").ap()
    wrs = nc.dram_tensor("WrS", [NHH, 128, C], F32, kind="ExternalInput").ap()
    wvs = nc.dram_tensor("WvS", [NCC, 128, H], F32, kind="ExternalInput").ap()
    w0s = nc.dram_tensor("W0S", [NNB, 128, C], F32, kind="ExternalInput").ap()
    w1s = nc.dram_tensor("W1S", [NCC, 128, NFF], F32, kind="ExternalInput").ap()
    tmk_d = nc.dram_tensor("tmk", [128, NCC], F32, kind="ExternalInput").ap()
    tmr_d = nc.dram_tensor("tmr", [128, NCC], F32, kind="ExternalInput").ap()
    dec_d = nc.dram_tensor("dec", [128, NHH], F32, kind="ExternalInput").ap()
    gam_d = nc.dram_tensor("gam", [128, NCC], F32, kind="ExternalInput").ap()
    bet_d = nc.dram_tensor("bet", [128, NCC], F32, kind="ExternalInput").ap()
    b0_d = nc.dram_tensor("b0", [128, NNB], F32, kind="ExternalInput").ap()
    b1_d = nc.dram_tensor("b1", [128, NCC], F32, kind="ExternalInput").ap()

    out_fm = nc.dram_tensor("out_fm", [C, OWN], F32, kind="ExternalOutput").ap()
    dbg = "ExternalOutput" if debug else "Internal"
    xn_sp = nc.dram_tensor("xn_sp", [C, OWN], F16, kind=dbg).ap()
    o_sp = nc.dram_tensor("o_sp", [H, OWN], F16, kind=dbg).ap()
    at_sp = nc.dram_tensor("at_sp", [C, OWN], F16, kind=dbg).ap()

    tmk = sb("tmk_t", [128, NCC], F32)
    tmr = sb("tmr_t", [128, NCC], F32)
    dec = sb("dec_t", [128, NHH], F32)
    gam = sb("gam_t", [128, NCC], F32)
    bet = sb("bet_t", [128, NCC], F32)
    b0 = sb("b0_t", [128, NNB], F32)
    b1 = sb("b1_t", [128, NCC], F32)
    selt = sb("sel_t", [128, 1], F32)
    wdec = sb("wdec_t", [128, NHH], F32)
    wtmp = sb("wtmp_t", [128, NHH], F32)
    epst = sb("eps_t", [128, 1], F32)
    neg1 = sb("neg1_t", [128, 1], F32)
    ones_col16 = sb("ones_col16", [128, 1], F16)
    ones_col32 = sb("ones_col32", [128, 1], F32)
    ones_row = sb("ones_row", [1, 128], F32)

    ab_es = contextlib.ExitStack()
    xk = ab_es.enter_context(nc.sbuf_tensor("xk_t", [128, NCC * T], F16))
    xr = ab_es.enter_context(nc.sbuf_tensor("xr_t", [128, NCC * OWN], F16))

    s_ld = sem("s_ld")
    s_w = sem("s_w")
    s_sp = sem("s_sp")
    s_pe = sem("s_pe")
    s_act = sem("s_act")
    s_dve = sem("s_dve")
    s_pl = sem("s_pl")

    def fin(close_ab):
        with nc.Block("finX") as blk:
            sp_all = cnt["s_sp"]

            @blk.gpsimd
            def _(g):
                g.wait_ge(s_sp, sp_all)
        if close_ab:
            ab_es.close()
        es.close()
        return nc

    with nc.Block("prep") as blk:
        @blk.sync
        def _(q):
            for dst, src in ((tmk, tmk_d), (tmr, tmr_d), (dec, dec_d), (gam, gam_d),
                             (bet, bet_d), (b0, b0_d), (b1, b1_d), (selt, sel)):
                q.dma_start(out=dst[:], in_=src).then_inc(s_ld, 16)
                cnt["s_ld"] += 16

        @blk.vector
        def _(v):
            v.memset(epst[:], EPS)
            v.memset(neg1[:], -1.0)
            v.memset(ones_col16[:], 1.0)
            v.memset(ones_col32[:], 1.0)
            v.memset(ones_row[:], 1.0).then_inc(s_dve, 1)
            cnt["s_dve"] += 1

        ld0, dve0 = cnt["s_ld"], cnt["s_dve"]

        @blk.scalar
        def _(sc):
            sc.wait_ge(s_ld, ld0)
            sc.wait_ge(s_dve, dve0)
            sc.activation(wtmp[:], dec[:], AF.Exp)
            sc.activation(wdec[:], wtmp[:], AF.Exp, scale=neg1[:, 0:1]).then_inc(s_act, 1)
            cnt["s_act"] += 1

    with contextlib.ExitStack() as pes:
        sbp = lambda *a, **k: pes.enter_context(nc.sbuf_tensor(*a, **k))
        psp = lambda n, s: pes.enter_context(nc.psum_tensor(n, s, F32))
        x_t = sbp("x_t", [128, NCC * 512], F32)
        xnt = sbp("xnt", [128, NCC * 513], F16)
        x2_t = sbp("x2_t", [128, 512], F32)
        xw_t = sbp("xw_t", [128, 512], F32)
        d_t = [sbp("d_tA", [128, 512], F16), sbp("d_tB", [128, 512], F16)]
        dcol = sbp("dcol", [128, 8], F32)
        m_row = sbp("m_row", [1, 512], F32)
        m2_row = sbp("m2_row", [1, 512], F32)
        v_row = sbp("v_row", [1, 512], F32)
        sd_row = sbp("sd_row", [1, 512], F32)
        a_row = sbp("a_row", [1, 512], F32)
        b_row = sbp("b_row", [1, 512], F32)
        t_row = sbp("t_row", [1, 512], F32)
        s1_ps = [psp("s1_psA", [1, 512]), psp("s1_psB", [1, 512])]
        s2_ps = [psp("s2_psA", [1, 512]), psp("s2_psB", [1, 512])]
        ab_ps = [psp("ab_psA", [128, 512]), psp("ab_psB", [128, 512])]
        bb_ps = [psp("bb_psA", [128, 512]), psp("bb_psB", [128, 512])]

        NT = len(TCOLS)
        A_ld = cnt["s_ld"]
        A_pe = cnt["s_pe"]
        A_act = cnt["s_act"]
        A_dve = cnt["s_dve"]
        A_pl = cnt["s_pl"]

        a_ld = lambda ti: A_ld + 256 * (ti + 1)
        a_pestat = lambda ti, cc: A_pe + ti * 17 + cc + 1
        a_pebc = lambda ti: A_pe + ti * 17 + 17
        a_sq = lambda ti, cc: A_act + ti * 17 + cc + 1
        a_sqrt = lambda ti: A_act + ti * 17 + 17
        a_var = lambda ti: A_dve + ti * 34 + 1
        a_rows = lambda ti: A_dve + ti * 34 + 2
        a_xn = lambda ti, cc: A_dve + ti * 34 + 2 + cc + 1
        a_mix = lambda ti, cc: A_dve + ti * 34 + 18 + cc + 1
        a_d = lambda ti, cc: A_pl + ti * 16 + cc + 1

        with nc.Block("phA") as blk:
            @blk.sync
            def _(q):
                def spills(ti):
                    t0, L = TCOLS[ti]
                    j0, j1 = max(t0, WARM), t0 + L
                    if j1 > j0:
                        a = j0 - t0
                        for cc in range(NCC):
                            q.wait_ge(s_dve, a_mix(ti, cc))
                            q.dma_start(
                                out=xn_sp[cc * 128:(cc + 1) * 128, j0 - WARM: j1 - WARM],
                                in_=xnt[:, cc * 513 + 1 + a: cc * 513 + 1 + a + (j1 - j0)],
                            ).then_inc(s_sp, 16)
                            cnt["s_sp"] += 16

                for ti in range(len(TCOLS) + 1):
                    if ti < len(TCOLS):
                        t0, L = TCOLS[ti]
                        if ti >= 1:
                            q.wait_ge(s_dve, a_xn(ti - 1, NCC - 1))
                        for cc in range(NCC):
                            q.dma_start(out=x_t[:, cc * 512: cc * 512 + L],
                                        in_=x_fm[cc * 128:(cc + 1) * 128, t0:t0 + L]).then_inc(s_ld, 16)
                            cnt["s_ld"] += 16
                    if ti >= 1:
                        spills(ti - 1)

            @blk.scalar
            def _(sc):
                for ti, (t0, L) in enumerate(TCOLS):
                    sc.wait_ge(s_ld, a_ld(ti))
                    for cc in range(NCC):
                        if cc > 0:
                            sc.wait_ge(s_pe, a_pestat(ti, cc - 1))
                        sc.activation(x2_t[:, 0:L], x_t[:, cc * 512: cc * 512 + L],
                                      AF.Square).then_inc(s_act, 1)
                        cnt["s_act"] += 1
                    sc.wait_ge(s_dve, a_var(ti))
                    sc.activation(sd_row[0:1, 0:L], v_row[0:1, 0:L], AF.Sqrt,
                                  bias=epst[0:1, 0:1]).then_inc(s_act, 1)
                    cnt["s_act"] += 1

            @blk.tensor
            def _(t):
                for ti, (t0, L) in enumerate(TCOLS):
                    par = ti % 2
                    t.wait_ge(s_ld, a_ld(ti))
                    if ti >= 2:
                        t.wait_ge(s_dve, a_rows(ti - 2))
                    for cc in range(NCC):
                        t.matmul(s1_ps[par][0:1, 0:L], ones_col32[:],
                                 x_t[:, cc * 512: cc * 512 + L],
                                 start=(cc == 0), stop=(cc == NCC - 1))
                        t.wait_ge(s_act, a_sq(ti, cc))
                        mm = t.matmul(s2_ps[par][0:1, 0:L], ones_col32[:],
                                      x2_t[:, 0:L],
                                      start=(cc == 0), stop=(cc == NCC - 1))
                        mm.then_inc(s_pe, 1)
                        cnt["s_pe"] += 1
                    t.wait_ge(s_dve, a_rows(ti))
                    if ti >= 2:
                        t.wait_ge(s_dve, a_xn(ti - 2, NCC - 1))
                    t.matmul(ab_ps[par][:, 0:L], ones_row[:], a_row[0:1, 0:L],
                             start=True, stop=True)
                    mm = t.matmul(bb_ps[par][:, 0:L], ones_row[:], b_row[0:1, 0:L],
                                  start=True, stop=True)
                    mm.then_inc(s_pe, 1)
                    cnt["s_pe"] += 1

            @blk.vector
            def _(v):
                for ti, (t0, L) in enumerate(TCOLS):
                    par = ti % 2
                    if ti > 0:
                        for cc in range(NCC):
                            v.tensor_copy(xnt[:, cc * 513: cc * 513 + 1],
                                          xnt[:, cc * 513 + 512: cc * 513 + 513])
                    else:
                        for cc in range(NCC):
                            v.memset(xnt[:, cc * 513: cc * 513 + 1], 0.0)
                    v.wait_ge(s_pe, a_pestat(ti, NCC - 1))
                    v.tensor_scalar_mul(m_row[0:1, 0:L], s1_ps[par][0:1, 0:L], 1.0 / C)
                    v.tensor_scalar_mul(m2_row[0:1, 0:L], s2_ps[par][0:1, 0:L], 1.0 / C)
                    v.tensor_mul(t_row[0:1, 0:L], m_row[0:1, 0:L], m_row[0:1, 0:L])
                    v.tensor_sub(v_row[0:1, 0:L], m2_row[0:1, 0:L],
                                 t_row[0:1, 0:L]).then_inc(s_dve, 1)
                    cnt["s_dve"] += 1
                    v.wait_ge(s_act, a_sqrt(ti))
                    v.reciprocal(a_row[0:1, 0:L], sd_row[0:1, 0:L])
                    v.scalar_tensor_tensor(b_row[0:1, 0:L], m_row[0:1, 0:L], -1.0,
                                           a_row[0:1, 0:L], ALU.mult,
                                           ALU.mult).then_inc(s_dve, 1)
                    cnt["s_dve"] += 1
                    v.wait_ge(s_pe, a_pebc(ti))
                    for cc in range(NCC):
                        v.tensor_mul(xw_t[:, 0:L], x_t[:, cc * 512: cc * 512 + L],
                                     ab_ps[par][:, 0:L])
                        v.tensor_add(xnt[:, cc * 513 + 1: cc * 513 + 1 + L],
                                     xw_t[:, 0:L], bb_ps[par][:, 0:L]).then_inc(s_dve, 1)
                        cnt["s_dve"] += 1
                    for cc in range(NCC):
                        v.wait_ge(s_pl, a_d(ti, cc))
                        base = cc * 513
                        dd = d_t[cc % 2]
                        if ti == 0:
                            kcol = xk[:, cc * T + 128: cc * T + 129]
                            rcol = xr[:, cc * OWN + 0: cc * OWN + 1]
                            v.tensor_sub(dcol[:, 0:1], xnt[:, base + 130:base + 131],
                                         xnt[:, base + 129:base + 130])
                        v.scalar_tensor_tensor(xk[:, cc * T + t0: cc * T + t0 + L],
                                               dd[:, 0:L], tmk[:, cc:cc + 1],
                                               xnt[:, base + 1:base + 1 + L],
                                               ALU.mult, ALU.add)
                        j0, j1 = max(t0, WARM), t0 + L
                        if j1 > j0:
                            a = j0 - t0
                            v.scalar_tensor_tensor(
                                xr[:, cc * OWN + j0 - WARM: cc * OWN + j1 - WARM],
                                dd[:, a:a + (j1 - j0)], tmr[:, cc:cc + 1],
                                xnt[:, base + 1 + a: base + 1 + a + (j1 - j0)],
                                ALU.mult, ALU.add)
                        if ti == 0:
                            v.scalar_tensor_tensor(dcol[:, 1:2], dcol[:, 0:1],
                                                   tmk[:, cc:cc + 1],
                                                   xnt[:, base + 129:base + 130],
                                                   ALU.mult, ALU.add)
                            v.scalar_tensor_tensor(dcol[:, 2:3], dcol[:, 0:1],
                                                   tmr[:, cc:cc + 1],
                                                   xnt[:, base + 129:base + 130],
                                                   ALU.mult, ALU.add)
                            v.tensor_sub(dcol[:, 3:4], dcol[:, 1:2], kcol)
                            v.tensor_sub(dcol[:, 4:5], dcol[:, 2:3], rcol)
                            v.scalar_tensor_tensor(dcol[:, 5:6], dcol[:, 3:4],
                                                   selt[:, 0:1], kcol,
                                                   ALU.mult, ALU.add)
                            v.scalar_tensor_tensor(dcol[:, 6:7], dcol[:, 4:5],
                                                   selt[:, 0:1], rcol,
                                                   ALU.mult, ALU.add)
                            v.tensor_copy(kcol, dcol[:, 5:6])
                            v.tensor_copy(rcol, dcol[:, 6:7])
                        v.engine_nop().then_inc(s_dve, 1)
                        cnt["s_dve"] += 1

            @blk.gpsimd
            def _(g):
                for ti, (t0, L) in enumerate(TCOLS):
                    for cc in range(NCC):
                        g.wait_ge(s_dve, a_xn(ti, cc))
                        if cc >= 2:
                            g.wait_ge(s_dve, a_mix(ti, cc - 2))
                        base = cc * 513
                        g.tensor_tensor(d_t[cc % 2][:, 0:L], xnt[:, base:base + L],
                                        xnt[:, base + 1:base + 1 + L], ALU.subtract)
                        if ti == 0:
                            g.memset(d_t[cc % 2][:, 0:1], 0.0)
                        g.engine_nop().then_inc(s_pl, 1)
                        cnt["s_pl"] += 1

        assert cnt["s_pe"] == A_pe + NT * 17
        assert cnt["s_act"] == A_act + NT * 17
        assert cnt["s_dve"] == A_dve + NT * 34
        assert cnt["s_pl"] == A_pl + NT * 16

    if debug:
        xk_db = nc.dram_tensor("xk_db", [128, NCC * T], F16, kind="ExternalOutput").ap()
        xr_db = nc.dram_tensor("xr_db", [128, NCC * OWN], F16, kind="ExternalOutput").ap()
        with nc.Block("dbgA") as blk:
            @blk.sync
            def _(q):
                q.dma_start(out=xk_db, in_=xk[:]).then_inc(s_sp, 16)
                q.dma_start(out=xr_db, in_=xr[:]).then_inc(s_sp, 16)
                cnt["s_sp"] += 32

    if "B" not in phases:
        return fin(close_ab=True)

    with contextlib.ExitStack() as pes:
        sbp = lambda *a, **k: pes.enter_context(nc.sbuf_tensor(*a, **k))
        psp = lambda n, s: pes.enter_context(nc.psum_tensor(n, s, F32))
        wk_sl = [sbp("wk_slA", [128, C], F16), sbp("wk_slB", [128, C], F16)]
        wr_sl = [sbp("wr_slA", [128, C], F16), sbp("wr_slB", [128, C], F16)]
        onesT = sbp("onesT", [128, T], F16)
        wb_t = [sbp("wb_tA", [128, T], F32, align_bytes=128),
                sbp("wb_tB", [128, T], F32, align_bytes=128)]
        s_scan = sbp("s_scan", [128, T], F32, align_bytes=128)
        pad_b = sbp("pad_b", [128, 16], F32)
        krelu = [sbp("kreluA", [128, T], F16, align_bytes=128),
                 sbp("kreluB", [128, T], F16, align_bytes=128)]
        r_t = [sbp("r_tA", [128, OWN], F16), sbp("r_tB", [128, OWN], F16)]
        o_t = [sbp("o_tA", [128, OWN], F16), sbp("o_tB", [128, OWN], F16)]
        k_ps = [psp("k_psA", [128, 512]), psp("k_psB", [128, 512])]
        r_ps = [psp("r_psA", [128, 512]), psp("r_psB", [128, 512])]

        with nc.Block("phB") as blk:
            sp_a_end = cnt["s_sp"]
            w_base = cnt["s_w"]
            pe_base = cnt["s_pe"]
            act_base = cnt["s_act"]
            dve_base = cnt["s_dve"]
            sp_base = cnt["s_sp"]
            do_k = "k" in bparts
            do_r = "r" in bparts
            do_s = "s" in bparts and do_k
            do_o = "o" in bparts and do_s and do_r
            groups = ([("k", tc) for tc in TCOLS] if do_k else []) + \
                     ([("r", rc) for rc in RCOLS] if do_r else [])
            ngi = len(groups)
            dve_per_h = (1 if do_s else 0) + (1 if do_o else 0)
            w_per_h = (16 if do_k else 0) + (16 if do_r else 0)

            @blk.gpsimd
            def _(g):
                g.wait_ge(s_sp, sp_a_end)
                for h in range(NHH):
                    if h >= 2 and ngi:
                        g.wait_ge(s_pe, pe_base + (h - 1) * ngi)
                    if do_k:
                        g.dma_start(out=wk_sl[h % 2][:], in_=wks[h, :, :]).then_inc(s_w, 16)
                        cnt["s_w"] += 16
                    if do_r:
                        g.dma_start(out=wr_sl[h % 2][:], in_=wrs[h, :, :]).then_inc(s_w, 16)
                        cnt["s_w"] += 16

            @blk.tensor
            def _(t):
                for h in range(NHH):
                    t.wait_ge(s_w, w_base + w_per_h * (h + 1))
                    for gi, (kind, (t0, L)) in enumerate(groups):
                        pp = gi % 2
                        need = h * ngi + gi - 1
                        if need > 0:
                            t.wait_ge(s_act, act_base + need)
                        for cc in range(NCC):
                            if kind == "k":
                                mm = t.matmul(k_ps[pp][:, 0:L],
                                              wk_sl[h % 2][:, cc * 128:(cc + 1) * 128],
                                              xk[:, cc * T + t0: cc * T + t0 + L],
                                              start=(cc == 0), stop=(cc == NCC - 1))
                            else:
                                mm = t.matmul(r_ps[pp][:, 0:L],
                                              wr_sl[h % 2][:, cc * 128:(cc + 1) * 128],
                                              xr[:, cc * OWN + (t0 - WARM): cc * OWN + (t0 - WARM) + L],
                                              start=(cc == 0), stop=(cc == NCC - 1))
                        mm.then_inc(s_pe, 1)
                        cnt["s_pe"] += 1

            @blk.scalar
            def _(sc):
                for h in range(NHH):
                    if h >= 2 and dve_per_h:
                        sc.wait_ge(s_dve, dve_base + dve_per_h * (h - 1))
                    for gi, (kind, (t0, L)) in enumerate(groups):
                        pp = gi % 2
                        sc.wait_ge(s_pe, pe_base + h * ngi + gi + 1)
                        if kind == "k":
                            sc.activation(krelu[h % 2][:, t0:t0 + L], k_ps[pp][:, 0:L],
                                          AF.Relu).then_inc(s_act, 1)
                        else:
                            sc.activation(r_t[h % 2][:, t0 - WARM: t0 - WARM + L],
                                          r_ps[pp][:, 0:L],
                                          AF.Copy).then_inc(s_act, 1)
                        cnt["s_act"] += 1

            @blk.vector
            def _(v):
                if dve_per_h:
                    v.memset(onesT[:], 1.0)
                    v.memset(s_scan[:], 0.0)
                    v.memset(wb_t[1][:], 0.0)
                    v.tensor_scalar_mul(wb_t[0][:], onesT[:], wdec[:, 0:1])
                    v.memset(s_scan[:, 0:8], 0.0)
                    v.memset(s_scan[:, 8:16], 0.0)
                    for h in range(NHH):
                        if h >= 2 and do_o:
                            v.wait_ge(s_sp, sp_base + 16 * (h - 1))
                        v.wait_ge(s_act, act_base + h * ngi + ngi)
                        v.tensor_tensor_scan(s_scan[:], wb_t[h % 2][:], krelu[h % 2][:], 0.0,
                                             ALU.mult, ALU.add).then_inc(s_dve, 1)
                        cnt["s_dve"] += 1

                        if h + 1 < NHH:
                            v.tensor_scalar_mul(wb_t[(h + 1) % 2][:], onesT[:],
                                                wdec[:, h + 1:h + 2])
                        if do_o:
                            v.wait_ge(s_act, act_base + h * ngi + 9)
                            v.tensor_mul(o_t[h % 2][:], s_scan[:, WARM:T], r_t[h % 2][:])
                            v.memset(wb_t[h % 2][:, 0:1], 1.0).then_inc(s_dve, 1)
                            cnt["s_dve"] += 1

            @blk.sync
            def _(q):
                if do_o:
                    for h in range(NHH):
                        q.wait_ge(s_dve, dve_base + dve_per_h * (h + 1))
                        q.dma_start(out=o_sp[h * 128:(h + 1) * 128, :],
                                    in_=o_t[h % 2][:]).then_inc(s_sp, 16)
                        cnt["s_sp"] += 16

    ab_es.close()

    if "C" not in phases:
        return fin(close_ab=False)

    with contextlib.ExitStack() as pes:
        sbp = lambda *a, **k: pes.enter_context(nc.sbuf_tensor(*a, **k))
        psp = lambda n, s: pes.enter_context(nc.psum_tensor(n, s, F32))
        o_all = sbp("o_all", [128, NHH * OWN], F16)
        v_all = sbp("v_all", [128, NCC * OWN], F16)
        wv_sl = [sbp("wv_slA", [128, H], F16), sbp("wv_slB", [128, H], F16),
                 sbp("wv_slC", [128, H], F16), sbp("wv_slD", [128, H], F16)]
        v2_t = [sbp("v2_tA", [128, 512], F16), sbp("v2_tB", [128, 512], F16)]
        scrA = [sbp("scrA0", [128, 512], F16), sbp("scrA1", [128, 512], F16)]
        scrB = [sbp("scrB0", [128, 512], F16), sbp("scrB1", [128, 512], F16)]
        ab2_s = sbp("ab2_s", [128, 512], F16)
        bb2_s = sbp("bb2_s", [128, 512], F16)
        m_row = sbp("m_rowC", [1, 512], F32)
        m2_row = sbp("m2_rowC", [1, 512], F32)
        t_row = sbp("t_rowC", [1, 512], F32)
        v_rowp = [sbp("v_rowC0", [1, 512], F32), sbp("v_rowC1", [1, 512], F32)]
        sd_rowp = [sbp("sd_rowC0", [1, 512], F32), sbp("sd_rowC1", [1, 512], F32)]
        a_rowp = [sbp("a_rowC0", [1, 512], F32), sbp("a_rowC1", [1, 512], F32)]
        b_rowp = [sbp("b_rowC0", [1, 512], F32), sbp("b_rowC1", [1, 512], F32)]
        v_ps = [psp("v_psA", [128, 512]), psp("v_psB", [128, 512])]
        s1_ps = [psp("s1_psA2", [1, 512]), psp("s1_psB2", [1, 512])]
        s2_ps = [psp("s2_psA2", [1, 512]), psp("s2_psB2", [1, 512])]
        ab_ps = psp("ab_psC", [128, 512])
        bb_ps = psp("bb_psC", [128, 512])

        sp_b_end = cnt["s_sp"]
        w_c_base = cnt["s_w"]
        pe_c_base = cnt["s_pe"]
        act_c_base = cnt["s_act"]
        dve_c_base = cnt["s_dve"]
        ld_o_base = cnt["s_ld"]

        def act_evac(p, co, t1):
            return act_c_base + p * 66 + (co * 2 + t1) * 2 + 1

        def act_square(p, co, t1):
            return act_c_base + p * 66 + (co * 2 + t1) * 2 + 2

        def act_sqrt(p, t1):
            return act_c_base + p * 66 + 64 + t1 + 1

        def pe_vgrp(p, co, t1):
            return pe_c_base + p * 66 + co * 4 + t1 + 1

        def pe_stats(p, co, t1):
            return pe_c_base + p * 66 + co * 4 + 2 + t1 + 1

        def pe_bcast(p, t1):
            return pe_c_base + p * 66 + 64 + t1 + 1

        def dve_var(p, t1):
            return dve_c_base + p * (2 + 2 * 18) + t1 + 1

        def dve_rowsfinal(p, t1):
            return dve_c_base + p * 38 + 2 + t1 * 18 + 1

        def dve_copies(p, t1):
            return dve_c_base + p * 38 + 2 + t1 * 18 + 2

        def dve_apply(p, t1, cc):
            return dve_c_base + p * 38 + 2 + t1 * 18 + 2 + cc + 1

        with nc.Block("phC") as blk:
            @blk.sync
            def _(q):
                q.wait_ge(s_sp, sp_b_end)
                for h in range(0, NHH, 2):
                    q.dma_start(out=o_all[:, h * OWN:(h + 1) * OWN],
                                in_=o_sp[h * 128:(h + 1) * 128, :]).then_inc(s_ld, 16)
                    cnt["s_ld"] += 16
                for p in range(2):
                    for t1 in range(2):
                        tj = 2 * p + t1
                        for cc in range(NCC):
                            q.wait_ge(s_dve, dve_apply(p, t1, cc))
                            sl = slice(cc * OWN + tj * 512, cc * OWN + tj * 512 + 512)
                            q.dma_start(out=at_sp[cc * 128:(cc + 1) * 128,
                                                  tj * 512:tj * 512 + 512],
                                        in_=v_all[:, sl]).then_inc(s_sp, 16)
                            cnt["s_sp"] += 16

            @blk.gpsimd
            def _(g):
                g.wait_ge(s_sp, sp_b_end)
                for idx in range(2 * NCC):
                    p, co = idx // NCC, idx % NCC
                    if idx >= 4:
                        g.wait_ge(s_pe, pe_vgrp((idx - 3) // NCC, (idx - 3) % NCC, 1))
                    g.dma_start(out=wv_sl[idx % 4][:],
                                in_=wvs[co, :, :]).then_inc(s_w, 16)
                    cnt["s_w"] += 16

            @blk.tensor
            def _(t):
                for p in range(2):
                    if p == 1:
                        t.wait_ge(s_dve, dve_var(0, 1))
                    for co in range(NCC):
                        idx = p * NCC + co
                        t.wait_ge(s_w, w_c_base + 16 * (idx + 1))
                        for t1 in range(2):
                            tj = 2 * p + t1
                            grp = idx * 2 + t1
                            pp = grp % 2
                            if grp >= 2:
                                gp = grp - 2
                                t.wait_ge(s_act, act_evac(gp // 32, (gp % 32) // 2, gp % 2))
                            if idx == 0 and t1 == 0:
                                t.wait_ge(s_ld, ld_o_base + 16 * NHH)
                            for hh in range(NHH):
                                mm = t.matmul(v_ps[pp][:, 0:512],
                                              wv_sl[idx % 4][:, hh * 128:(hh + 1) * 128],
                                              o_all[:, hh * OWN + tj * 512: hh * OWN + tj * 512 + 512],
                                              start=(hh == 0), stop=(hh == NHH - 1))
                            mm.then_inc(s_pe, 1)
                            cnt["s_pe"] += 1
                        for t1 in range(2):
                            t.wait_ge(s_act, act_square(p, co, t1))
                            tj = 2 * p + t1
                            t.matmul(s1_ps[t1][0:1, :], ones_col16[:],
                                     v_all[:, co * OWN + tj * 512: co * OWN + tj * 512 + 512],
                                     start=(co == 0), stop=(co == NCC - 1))
                            mm = t.matmul(s2_ps[t1][0:1, :], ones_col16[:],
                                          v2_t[t1][:],
                                          start=(co == 0), stop=(co == NCC - 1))
                            mm.then_inc(s_pe, 1)
                            cnt["s_pe"] += 1
                    for t1 in range(2):
                        t.wait_ge(s_dve, dve_rowsfinal(p, t1))
                        if t1 == 1:
                            t.wait_ge(s_dve, dve_copies(p, 0))
                        elif p == 1:
                            t.wait_ge(s_dve, dve_copies(0, 1))
                        t.matmul(ab_ps[:], ones_row[:], a_rowp[t1][0:1, :],
                                 start=True, stop=True)
                        mm = t.matmul(bb_ps[:], ones_row[:], b_rowp[t1][0:1, :],
                                      start=True, stop=True)
                        mm.then_inc(s_pe, 1)
                        cnt["s_pe"] += 1

            @blk.scalar
            def _(sc):
                sc.wait_ge(s_sp, sp_b_end)
                for h in range(1, NHH, 2):
                    sc.dma_start(out=o_all[:, h * OWN:(h + 1) * OWN],
                                 in_=o_sp[h * 128:(h + 1) * 128, :]).then_inc(s_ld, 16)
                    cnt["s_ld"] += 16
                for p in range(2):
                    for co in range(NCC):
                        for t1 in range(2):
                            tj = 2 * p + t1
                            grp = (p * NCC + co) * 2 + t1
                            pp = grp % 2
                            sc.wait_ge(s_pe, pe_vgrp(p, co, t1))
                            if co >= 1:
                                sc.wait_ge(s_pe, pe_stats(p, co - 1, t1))
                            sl = slice(co * OWN + tj * 512, co * OWN + tj * 512 + 512)
                            sc.activation(v_all[:, sl], v_ps[pp][:, 0:512],
                                          AF.Copy).then_inc(s_act, 1)
                            cnt["s_act"] += 1
                            sc.activation(v2_t[t1][:], v_all[:, sl],
                                          AF.Square).then_inc(s_act, 1)
                            cnt["s_act"] += 1
                    for t1 in range(2):
                        sc.wait_ge(s_dve, dve_var(p, t1))
                        sc.activation(sd_rowp[t1][0:1, :], v_rowp[t1][0:1, :], AF.Sqrt,
                                      bias=epst[0:1, 0:1]).then_inc(s_act, 1)
                        cnt["s_act"] += 1

            @blk.vector
            def _(v):
                for p in range(2):
                    for t1 in range(2):
                        v.wait_ge(s_pe, pe_stats(p, NCC - 1, t1))
                        v.tensor_scalar_mul(m_row[0:1, :], s1_ps[t1][0:1, :], 1.0 / C)
                        v.tensor_scalar_mul(m2_row[0:1, :], s2_ps[t1][0:1, :], 1.0 / C)
                        v.tensor_mul(t_row[0:1, :], m_row[0:1, :], m_row[0:1, :])
                        v.tensor_sub(v_rowp[t1][0:1, :], m2_row[0:1, :],
                                     t_row[0:1, :]).then_inc(s_dve, 1)
                        cnt["s_dve"] += 1
                        v.tensor_scalar_mul(b_rowp[t1][0:1, :], m_row[0:1, :], -1.0)
                    for t1 in range(2):
                        tj = 2 * p + t1
                        v.wait_ge(s_act, act_sqrt(p, t1))
                        v.reciprocal(a_rowp[t1][0:1, :], sd_rowp[t1][0:1, :])
                        v.tensor_copy(t_row[0:1, :],
                                      a_rowp[t1][0:1, :]).then_inc(s_dve, 1)
                        cnt["s_dve"] += 1
                        v.wait_ge(s_pe, pe_bcast(p, t1))
                        v.tensor_copy(ab2_s[:], ab_ps[:])
                        v.tensor_copy(bb2_s[:], bb_ps[:]).then_inc(s_dve, 1)
                        cnt["s_dve"] += 1
                        def _sl(cc):
                            return slice(cc * OWN + tj * 512, cc * OWN + tj * 512 + 512)
                        for cc in range(NCC + 2):
                            if cc < NCC:
                                v.tensor_add(scrA[cc % 2][:], v_all[:, _sl(cc)], bb2_s[:])
                            if 1 <= cc <= NCC:
                                v.tensor_mul(scrB[(cc - 1) % 2][:],
                                             scrA[(cc - 1) % 2][:], ab2_s[:])
                            if cc >= 2:
                                v.tensor_scalar(v_all[:, _sl(cc - 2)],
                                                scrB[(cc - 2) % 2][:],
                                                gam[:, cc - 2:cc - 1],
                                                bet[:, cc - 2:cc - 1], ALU.mult,
                                                ALU.add).then_inc(s_dve, 1)
                                cnt["s_dve"] += 1
                        v.memset(scrA[0][:, 0:1], 0.0)

        assert cnt["s_pe"] == pe_c_base + 2 * 66, (cnt["s_pe"], pe_c_base)
        assert cnt["s_act"] == act_c_base + 2 * 66
        assert cnt["s_dve"] == dve_c_base + 2 * 38

    if "D" not in phases:
        return fin(close_ab=False)

    with contextlib.ExitStack() as pes:
        sbp = lambda *a, **k: pes.enter_context(nc.sbuf_tensor(*a, **k))
        psp = lambda n, s: pes.enter_context(nc.psum_tensor(n, s, F32))
        g_all = sbp("g_all", [128, NNB * 1024], F16)
        xn_tt = sbp("xn_tt", [128, NCC * 1024], F16)
        w0_sl = [sbp("w0_slA", [128, C], F16), sbp("w0_slB", [128, C], F16)]
        w1_sl = [sbp("w1_slA", [128, 2048], F16), sbp("w1_slB", [128, 2048], F16)]
        xc_t = [sbp("xc_tA", [128, 1024], F32), sbp("xc_tB", [128, 1024], F32)]
        at_t = [sbp("at_tA", [128, 1024], F16), sbp("at_tB", [128, 1024], F16)]
        ffn_t = [sbp("ffn_tA", [128, 1024], F16), sbp("ffn_tB", [128, 1024], F16)]
        ou_t = sbp("ou_t", [128, 1024], F32)
        d_ps = [psp("d_psA", [128, 1024]), psp("d_psB", [128, 1024])]
        e_ps = [psp("e_psA", [128, 1024]), psp("e_psB", [128, 1024])]

        with nc.Block("phD") as blk:
          d_prev = {}
          for di, (j0d, Lt) in enumerate(OCOLS2):
                sp_prev = cnt["s_sp"]
                w_base = cnt["s_w"]
                pe_base = cnt["s_pe"]
                act_base = cnt["s_act"]
                res_base = cnt["s_dve"]
                pe_d2_base = pe_base + NNB

                ld_xn_base = cnt["s_ld"]

                @blk.sync
                def _(q, j0d=j0d):
                    q.wait_ge(s_sp, sp_prev)
                    for cc in range(NCC):
                        q.dma_start(out=xn_tt[:, cc * 1024:(cc + 1) * 1024],
                                    in_=xn_sp[cc * 128:(cc + 1) * 128, j0d:j0d + 1024]).then_inc(s_ld, 16)
                        cnt["s_ld"] += 16

                @blk.gpsimd
                def _(g):
                    g.wait_ge(s_sp, sp_prev)
                    for nb in range(NNB):
                        if nb >= 2:
                            g.wait_ge(s_pe, pe_base + nb - 1)
                        g.dma_start(out=w0_sl[nb % 2][:], in_=w0s[nb, :, :]).then_inc(s_w, 16)
                        cnt["s_w"] += 16
                    for idx in range(4 * NCC):
                        co, qt = idx // 4, idx % 4
                        if idx >= 2:
                            g.wait_ge(s_pe, pe_d2_base + idx - 1)
                        g.dma_start(out=w1_sl[idx % 2][:],
                                    in_=w1s[co, :, qt * 2048:(qt + 1) * 2048]).then_inc(s_w, 16)
                        cnt["s_w"] += 16

                @blk.tensor
                def _(t, di=di):
                    for nb in range(NNB):
                        t.wait_ge(s_w, w_base + 16 * (nb + 1))
                        pp = nb % 2
                        if nb >= 2:
                            t.wait_ge(s_act, act_base + nb - 1)
                        elif di == 1:
                            t.wait_ge(s_act, act_base - NCC)
                        if nb == 0:
                            t.wait_ge(s_ld, ld_xn_base + 16 * NCC)
                        for cc in range(NCC):
                            t.matmul(d_ps[pp][:, 0:512],
                                     w0_sl[pp][:, cc * 128:(cc + 1) * 128],
                                     xn_tt[:, cc * 1024: cc * 1024 + 512],
                                     start=(cc == 0), stop=False)
                            mm = t.matmul(d_ps[pp][:, 512:1024],
                                          w0_sl[pp][:, cc * 128:(cc + 1) * 128],
                                          xn_tt[:, cc * 1024 + 512: cc * 1024 + 1024],
                                          start=(cc == 0), stop=(cc == NCC - 1))
                        mm.then_inc(s_pe, 1)
                        cnt["s_pe"] += 1
                    t.wait_ge(s_act, act_base + NNB)
                    for co in range(NCC):
                        pp = co % 2
                        if co >= 1:
                            t.wait_ge(s_act, act_base + NNB + max(co - 1, 0))
                        elif di == 1:
                            t.wait_ge(s_act, act_base)
                        for qt in range(4):
                            t.wait_ge(s_w, w_base + 16 * (NNB + co * 4 + qt + 1))
                            for nn in range(16):
                                nb = qt * 16 + nn
                                t.matmul(e_ps[pp][:, 0:512],
                                         w1_sl[(co * 4 + qt) % 2][:, nn * 128:(nn + 1) * 128],
                                         g_all[:, nb * 1024: nb * 1024 + 512],
                                         start=(nb == 0), stop=False)
                                mm = t.matmul(e_ps[pp][:, 512:1024],
                                              w1_sl[(co * 4 + qt) % 2][:, nn * 128:(nn + 1) * 128],
                                              g_all[:, nb * 1024 + 512: nb * 1024 + 1024],
                                              start=(nb == 0), stop=(nb == NNB - 1))
                            mm.then_inc(s_pe, 1)
                            cnt["s_pe"] += 1

                @blk.scalar
                def _(sc, di=di):
                    for nb in range(NNB):
                        sc.wait_ge(s_pe, pe_base + nb + 1)
                        if nb == 0 and di == 1:
                            sc.wait_ge(s_pe, pe_base)
                        sc.activation(g_all[:, nb * 1024:(nb + 1) * 1024],
                                      d_ps[nb % 2][:], AF.Gelu,
                                      bias=b0[:, nb:nb + 1]).then_inc(s_act, 1)
                        cnt["s_act"] += 1
                    for co in range(NCC):
                        sc.wait_ge(s_pe, pe_d2_base + 4 * (co + 1))
                        if co >= 2:
                            sc.wait_ge(s_dve, res_base + co - 1)
                        elif di == 1:
                            sc.wait_ge(s_dve, res_base)
                        sc.activation(ffn_t[co % 2][:], e_ps[co % 2][:], AF.Identity,
                                      bias=b1[:, co:co + 1]).then_inc(s_act, 1)
                        cnt["s_act"] += 1

                ld2_base = cnt["s_ld"]
                sp_out_base = cnt["s_sp"]

                @blk.sync
                def _(q, j0d=j0d):
                    for co in range(NCC):
                        if co >= 2:
                            q.wait_ge(s_dve, res_base + co - 1)
                            q.wait_ge(s_sp, sp_out_base + 16 * (co - 1))
                        q.dma_start(out=xc_t[co % 2][:],
                                    in_=x_fm[co * 128:(co + 1) * 128,
                                             WARM + j0d: WARM + j0d + 1024]).then_inc(s_ld, 16)
                        q.dma_start(out=at_t[co % 2][:],
                                    in_=at_sp[co * 128:(co + 1) * 128, j0d:j0d + 1024]).then_inc(s_ld, 16)
                        cnt["s_ld"] += 32
                        if co >= 1:
                            q.wait_ge(s_dve, res_base + co)
                            q.dma_start(out=out_fm[(co - 1) * 128:co * 128, j0d:j0d + 1024],
                                        in_=xc_t[(co - 1) % 2][:]).then_inc(s_sp, 16)
                            cnt["s_sp"] += 16
                    q.wait_ge(s_dve, res_base + NCC)
                    q.dma_start(out=out_fm[(NCC - 1) * 128:NCC * 128, j0d:j0d + 1024],
                                in_=xc_t[(NCC - 1) % 2][:]).then_inc(s_sp, 16)
                    cnt["s_sp"] += 16

                @blk.vector
                def _(v):
                    for co in range(NCC):
                        v.wait_ge(s_ld, ld2_base + 32 * (co + 1))
                        v.wait_ge(s_act, act_base + NNB + co + 1)
                        v.tensor_add(ou_t[:], xc_t[co % 2][:], at_t[co % 2][:])
                        v.tensor_add(xc_t[co % 2][:], ou_t[:],
                                     ffn_t[co % 2][:]).then_inc(s_dve, 1)
                        cnt["s_dve"] += 1

    return fin(close_ab=False)



def _host_layouts(inputs):
    f32 = np.float32
    Wk = np.ascontiguousarray(np.asarray(inputs["L0_attn_Wk"], f32))
    Wr = np.ascontiguousarray(np.asarray(inputs["L0_attn_Wr"], f32))
    Wv = np.ascontiguousarray(np.asarray(inputs["L0_attn_Wv"], f32))
    W0 = np.ascontiguousarray(np.asarray(inputs["L0_ffn_W0"], f32))
    W1 = np.ascontiguousarray(np.asarray(inputs["L0_ffn_W1"], f32))
    shared = {
        "WkS": np.ascontiguousarray(Wk.reshape(NHH, 128, NCC, 128).transpose(0, 3, 2, 1)).reshape(NHH, 128, C),
        "WrS": np.ascontiguousarray(Wr.reshape(NHH, 128, NCC, 128).transpose(0, 3, 2, 1)).reshape(NHH, 128, C),
        "WvS": np.ascontiguousarray(Wv.reshape(NCC, 128, NHH, 128).transpose(0, 3, 2, 1)).reshape(NCC, 128, H),
        "W0S": np.ascontiguousarray(W0.reshape(NNB, 128, NCC, 128).transpose(0, 3, 2, 1)).reshape(NNB, 128, C),
        "W1S": np.ascontiguousarray(W1.reshape(NCC, 128, NNB, 128).transpose(0, 3, 2, 1)).reshape(NCC, 128, NFF),
        "tmk": np.ascontiguousarray(np.asarray(inputs["L0_attn_time_mix_k"], f32).reshape(NCC, 128).T),
        "tmr": np.ascontiguousarray(np.asarray(inputs["L0_attn_time_mix_r"], f32).reshape(NCC, 128).T),
        "dec": np.ascontiguousarray(np.asarray(inputs["L0_attn_time_decay"], f32).reshape(NHH, 128).T),
        "gam": np.ascontiguousarray(np.asarray(inputs["L0_attn_ln_gamma"], f32).reshape(NCC, 128).T),
        "bet": np.ascontiguousarray(np.asarray(inputs["L0_attn_ln_beta"], f32).reshape(NCC, 128).T),
        "b0": np.ascontiguousarray(np.asarray(inputs["L0_ffn_bias0"], f32).reshape(NNB, 128).T),
        "b1": np.ascontiguousarray(np.asarray(inputs["L0_ffn_bias1"], f32).reshape(NCC, 128).T),
    }
    return shared


def make_in_maps(inputs):
    x = np.asarray(inputs["L0_attn_x"], np.float32)
    shared = _host_layouts(inputs)
    in_maps = []
    for c in range(8):
        b, half = c // 2, c % 2
        x_fm = np.zeros((C, T), np.float32)
        if half == 0:
            x_fm[:, WARM:] = x[b, 0:OWN].T
        else:
            x_fm[:, :] = x[b, OWN - WARM: 2 * OWN].T
        m = dict(shared)
        m["x_fm"] = np.ascontiguousarray(x_fm)
        m["sel"] = np.full((128, 1), 1.0 if half == 0 else 0.0, np.float32)
        in_maps.append(m)
    return in_maps


def kernel(**inputs):
    nc = build_program(debug=False)
    in_maps = make_in_maps(inputs)
    res = run_bass_kernel_spmd(nc, in_maps, core_ids=list(range(8)))
    out = np.empty((4, 4096, C), np.float32)
    for c, r in enumerate(res.results):
        b, half = c // 2, c % 2
        out[b, half * OWN:(half + 1) * OWN, :] = r["out_fm"].T
    return out


# version 09
# speedup vs baseline: 1.0627x; 1.0627x over previous
import contextlib
import numpy as np

import concourse.bass as bass
import concourse.mybir as mybir
from concourse.bass_utils import run_bass_kernel_spmd

F32 = mybir.dt.float32
F16 = mybir.dt.float16
AF = mybir.ActivationFunctionType
ALU = mybir.AluOpType

C = 2048
H = 2816
NFF = 8192
NCC = C // 128
NHH = H // 128
NNB = NFF // 128
WARM = 128
OWN = 2048
T = WARM + OWN
EPS = 1e-5

TCOLS = [(0, 512), (512, 512), (1024, 512), (1536, 512), (2048, 128)]
RCOLS = [(128, 512), (640, 512), (1152, 512), (1664, 512)]
OCOLS2 = [(0, 1024), (1024, 1024)]


def build_program(debug=False, phases="ABCD", bparts="krso"):
    nc = bass.Bass("TRN2", target_bir_lowering=False,
                   detect_race_conditions=False)
    es = contextlib.ExitStack()
    cnt = {}

    def sem(name):
        h = es.enter_context(nc.semaphore(name))
        cnt[name] = 0
        return h

    def sb(name, shape, dtype, **kw):
        return es.enter_context(nc.sbuf_tensor(name, shape, dtype, **kw))

    x_fm = nc.dram_tensor("x_fm", [C, T], F32, kind="ExternalInput").ap()
    sel = nc.dram_tensor("sel", [128, 1], F32, kind="ExternalInput").ap()
    wks = nc.dram_tensor("WkS", [NHH, 128, C], F32, kind="ExternalInput").ap()
    wrs = nc.dram_tensor("WrS", [NHH, 128, C], F32, kind="ExternalInput").ap()
    wvs = nc.dram_tensor("WvS", [NCC, 128, H], F32, kind="ExternalInput").ap()
    w0s = nc.dram_tensor("W0S", [NNB, 128, C], F32, kind="ExternalInput").ap()
    w1s = nc.dram_tensor("W1S", [NCC, 128, NFF], F32, kind="ExternalInput").ap()
    tmk_d = nc.dram_tensor("tmk", [128, NCC], F32, kind="ExternalInput").ap()
    tmr_d = nc.dram_tensor("tmr", [128, NCC], F32, kind="ExternalInput").ap()
    dec_d = nc.dram_tensor("dec", [128, NHH], F32, kind="ExternalInput").ap()
    gam_d = nc.dram_tensor("gam", [128, NCC], F32, kind="ExternalInput").ap()
    bet_d = nc.dram_tensor("bet", [128, NCC], F32, kind="ExternalInput").ap()
    b0_d = nc.dram_tensor("b0", [128, NNB], F32, kind="ExternalInput").ap()
    b1_d = nc.dram_tensor("b1", [128, NCC], F32, kind="ExternalInput").ap()

    out_fm = nc.dram_tensor("out_fm", [C, OWN], F32, kind="ExternalOutput").ap()
    dbg = "ExternalOutput" if debug else "Internal"
    xn_sp = nc.dram_tensor("xn_sp", [C, OWN], F16, kind=dbg).ap()
    o_sp = nc.dram_tensor("o_sp", [H, OWN], F16, kind=dbg).ap()
    at_sp = nc.dram_tensor("at_sp", [C, OWN], F16, kind=dbg).ap()

    tmk = sb("tmk_t", [128, NCC], F32)
    tmr = sb("tmr_t", [128, NCC], F32)
    dec = sb("dec_t", [128, NHH], F32)
    gam = sb("gam_t", [128, NCC], F32)
    bet = sb("bet_t", [128, NCC], F32)
    b0 = sb("b0_t", [128, NNB], F32)
    b1 = sb("b1_t", [128, NCC], F32)
    selt = sb("sel_t", [128, 1], F32)
    wdec = sb("wdec_t", [128, NHH], F32)
    wtmp = sb("wtmp_t", [128, NHH], F32)
    epst = sb("eps_t", [128, 1], F32)
    neg1 = sb("neg1_t", [128, 1], F32)
    ones_col16 = sb("ones_col16", [128, 1], F16)
    ones_col32 = sb("ones_col32", [128, 1], F32)
    ones_row = sb("ones_row", [1, 128], F32)

    ab_es = contextlib.ExitStack()
    xk = ab_es.enter_context(nc.sbuf_tensor("xk_t", [128, NCC * T], F16))
    xr = ab_es.enter_context(nc.sbuf_tensor("xr_t", [128, NCC * OWN], F16))

    s_ld = sem("s_ld")
    s_w = sem("s_w")
    s_sp = sem("s_sp")
    s_pe = sem("s_pe")
    s_act = sem("s_act")
    s_dve = sem("s_dve")
    s_pl = sem("s_pl")

    def fin(close_ab):
        with nc.Block("finX") as blk:
            sp_all = cnt["s_sp"]

            @blk.gpsimd
            def _(g):
                g.wait_ge(s_sp, sp_all)
        if close_ab:
            ab_es.close()
        es.close()
        return nc

    with nc.Block("prep") as blk:
        @blk.sync
        def _(q):
            for dst, src in ((tmk, tmk_d), (tmr, tmr_d), (dec, dec_d), (gam, gam_d),
                             (bet, bet_d), (b0, b0_d), (b1, b1_d), (selt, sel)):
                q.dma_start(out=dst[:], in_=src).then_inc(s_ld, 16)
                cnt["s_ld"] += 16

        @blk.vector
        def _(v):
            v.memset(epst[:], EPS)
            v.memset(neg1[:], -1.0)
            v.memset(ones_col16[:], 1.0)
            v.memset(ones_col32[:], 1.0)
            v.memset(ones_row[:], 1.0).then_inc(s_dve, 1)
            cnt["s_dve"] += 1

        ld0, dve0 = cnt["s_ld"], cnt["s_dve"]

        @blk.scalar
        def _(sc):
            sc.wait_ge(s_ld, ld0)
            sc.wait_ge(s_dve, dve0)
            sc.activation(wtmp[:], dec[:], AF.Exp)
            sc.activation(wdec[:], wtmp[:], AF.Exp, scale=neg1[:, 0:1]).then_inc(s_act, 1)
            cnt["s_act"] += 1

    with contextlib.ExitStack() as pes:
        sbp = lambda *a, **k: pes.enter_context(nc.sbuf_tensor(*a, **k))
        psp = lambda n, s: pes.enter_context(nc.psum_tensor(n, s, F32))
        x_t = sbp("x_t", [128, NCC * 512], F32)
        xnt = sbp("xnt", [128, NCC * 513], F16)
        x2_t = sbp("x2_t", [128, 512], F32)
        xw_t = sbp("xw_t", [128, 512], F32)
        d_t = [sbp("d_tA", [128, 512], F16), sbp("d_tB", [128, 512], F16)]
        dcol = sbp("dcol", [128, 8], F32)
        m_row = sbp("m_row", [1, 512], F32)
        m2_row = sbp("m2_row", [1, 512], F32)
        v_row = sbp("v_row", [1, 512], F32)
        sd_row = sbp("sd_row", [1, 512], F32)
        a_row = sbp("a_row", [1, 512], F32)
        b_row = sbp("b_row", [1, 512], F32)
        t_row = sbp("t_row", [1, 512], F32)
        s1_ps = [psp("s1_psA", [1, 512]), psp("s1_psB", [1, 512])]
        s2_ps = [psp("s2_psA", [1, 512]), psp("s2_psB", [1, 512])]
        ab_ps = [psp("ab_psA", [128, 512]), psp("ab_psB", [128, 512])]
        bb_ps = [psp("bb_psA", [128, 512]), psp("bb_psB", [128, 512])]

        NT = len(TCOLS)
        A_ld = cnt["s_ld"]
        A_pe = cnt["s_pe"]
        A_act = cnt["s_act"]
        A_dve = cnt["s_dve"]
        A_pl = cnt["s_pl"]

        a_ld = lambda ti: A_ld + 256 * (ti + 1)
        a_pestat = lambda ti, cc: A_pe + ti * 17 + cc + 1
        a_pebc = lambda ti: A_pe + ti * 17 + 17
        a_sq = lambda ti, cc: A_act + ti * 17 + cc + 1
        a_sqrt = lambda ti: A_act + ti * 17 + 17
        a_var = lambda ti: A_dve + ti * 34 + 1
        a_rows = lambda ti: A_dve + ti * 34 + 2
        a_xn = lambda ti, cc: A_dve + ti * 34 + 2 + cc + 1
        a_mix = lambda ti, cc: A_dve + ti * 34 + 18 + cc + 1
        a_d = lambda ti, cc: A_pl + ti * 16 + cc + 1

        with nc.Block("phA") as blk:
            @blk.sync
            def _(q):
                def spills(ti):
                    t0, L = TCOLS[ti]
                    j0, j1 = max(t0, WARM), t0 + L
                    if j1 > j0:
                        a = j0 - t0
                        for cc in range(NCC):
                            q.wait_ge(s_dve, a_mix(ti, cc))
                            q.dma_start(
                                out=xn_sp[cc * 128:(cc + 1) * 128, j0 - WARM: j1 - WARM],
                                in_=xnt[:, cc * 513 + 1 + a: cc * 513 + 1 + a + (j1 - j0)],
                            ).then_inc(s_sp, 16)
                            cnt["s_sp"] += 16

                for ti in range(len(TCOLS) + 1):
                    if ti < len(TCOLS):
                        t0, L = TCOLS[ti]
                        if ti >= 1:
                            q.wait_ge(s_dve, a_xn(ti - 1, NCC - 1))
                        for cc in range(NCC):
                            q.dma_start(out=x_t[:, cc * 512: cc * 512 + L],
                                        in_=x_fm[cc * 128:(cc + 1) * 128, t0:t0 + L]).then_inc(s_ld, 16)
                            cnt["s_ld"] += 16
                    if ti >= 1:
                        spills(ti - 1)

            @blk.scalar
            def _(sc):
                for ti, (t0, L) in enumerate(TCOLS):
                    sc.wait_ge(s_ld, a_ld(ti))
                    for cc in range(NCC):
                        if cc > 0:
                            sc.wait_ge(s_pe, a_pestat(ti, cc - 1))
                        sc.activation(x2_t[:, 0:L], x_t[:, cc * 512: cc * 512 + L],
                                      AF.Square).then_inc(s_act, 1)
                        cnt["s_act"] += 1
                    sc.wait_ge(s_dve, a_var(ti))
                    sc.activation(sd_row[0:1, 0:L], v_row[0:1, 0:L], AF.Sqrt,
                                  bias=epst[0:1, 0:1]).then_inc(s_act, 1)
                    cnt["s_act"] += 1

            @blk.tensor
            def _(t):
                for ti, (t0, L) in enumerate(TCOLS):
                    par = ti % 2
                    t.wait_ge(s_ld, a_ld(ti))
                    if ti >= 2:
                        t.wait_ge(s_dve, a_rows(ti - 2))
                    for cc in range(NCC):
                        t.matmul(s1_ps[par][0:1, 0:L], ones_col32[:],
                                 x_t[:, cc * 512: cc * 512 + L],
                                 start=(cc == 0), stop=(cc == NCC - 1))
                        t.wait_ge(s_act, a_sq(ti, cc))
                        mm = t.matmul(s2_ps[par][0:1, 0:L], ones_col32[:],
                                      x2_t[:, 0:L],
                                      start=(cc == 0), stop=(cc == NCC - 1))
                        mm.then_inc(s_pe, 1)
                        cnt["s_pe"] += 1
                    t.wait_ge(s_dve, a_rows(ti))
                    if ti >= 2:
                        t.wait_ge(s_dve, a_xn(ti - 2, NCC - 1))
                    t.matmul(ab_ps[par][:, 0:L], ones_row[:], a_row[0:1, 0:L],
                             start=True, stop=True)
                    mm = t.matmul(bb_ps[par][:, 0:L], ones_row[:], b_row[0:1, 0:L],
                                  start=True, stop=True)
                    mm.then_inc(s_pe, 1)
                    cnt["s_pe"] += 1

            @blk.vector
            def _(v):
                for ti, (t0, L) in enumerate(TCOLS):
                    par = ti % 2
                    if ti > 0:
                        for cc in range(NCC):
                            v.tensor_copy(xnt[:, cc * 513: cc * 513 + 1],
                                          xnt[:, cc * 513 + 512: cc * 513 + 513])
                    else:
                        for cc in range(NCC):
                            v.memset(xnt[:, cc * 513: cc * 513 + 1], 0.0)
                    v.wait_ge(s_pe, a_pestat(ti, NCC - 1))
                    v.tensor_scalar_mul(m_row[0:1, 0:L], s1_ps[par][0:1, 0:L], 1.0 / C)
                    v.tensor_scalar_mul(m2_row[0:1, 0:L], s2_ps[par][0:1, 0:L], 1.0 / C)
                    v.tensor_mul(t_row[0:1, 0:L], m_row[0:1, 0:L], m_row[0:1, 0:L])
                    v.tensor_sub(v_row[0:1, 0:L], m2_row[0:1, 0:L],
                                 t_row[0:1, 0:L]).then_inc(s_dve, 1)
                    cnt["s_dve"] += 1
                    v.wait_ge(s_act, a_sqrt(ti))
                    v.reciprocal(a_row[0:1, 0:L], sd_row[0:1, 0:L])
                    v.scalar_tensor_tensor(b_row[0:1, 0:L], m_row[0:1, 0:L], -1.0,
                                           a_row[0:1, 0:L], ALU.mult,
                                           ALU.mult).then_inc(s_dve, 1)
                    cnt["s_dve"] += 1
                    v.wait_ge(s_pe, a_pebc(ti))
                    for cc in range(NCC):
                        v.tensor_mul(xw_t[:, 0:L], x_t[:, cc * 512: cc * 512 + L],
                                     ab_ps[par][:, 0:L])
                        v.tensor_add(xnt[:, cc * 513 + 1: cc * 513 + 1 + L],
                                     xw_t[:, 0:L], bb_ps[par][:, 0:L]).then_inc(s_dve, 1)
                        cnt["s_dve"] += 1
                    for cc in range(NCC):
                        v.wait_ge(s_pl, a_d(ti, cc))
                        base = cc * 513
                        dd = d_t[cc % 2]
                        if ti == 0:
                            kcol = xk[:, cc * T + 128: cc * T + 129]
                            rcol = xr[:, cc * OWN + 0: cc * OWN + 1]
                            v.tensor_sub(dcol[:, 0:1], xnt[:, base + 130:base + 131],
                                         xnt[:, base + 129:base + 130])
                        v.scalar_tensor_tensor(xk[:, cc * T + t0: cc * T + t0 + L],
                                               dd[:, 0:L], tmk[:, cc:cc + 1],
                                               xnt[:, base + 1:base + 1 + L],
                                               ALU.mult, ALU.add)
                        j0, j1 = max(t0, WARM), t0 + L
                        if j1 > j0:
                            a = j0 - t0
                            v.scalar_tensor_tensor(
                                xr[:, cc * OWN + j0 - WARM: cc * OWN + j1 - WARM],
                                dd[:, a:a + (j1 - j0)], tmr[:, cc:cc + 1],
                                xnt[:, base + 1 + a: base + 1 + a + (j1 - j0)],
                                ALU.mult, ALU.add)
                        if ti == 0:
                            v.scalar_tensor_tensor(dcol[:, 1:2], dcol[:, 0:1],
                                                   tmk[:, cc:cc + 1],
                                                   xnt[:, base + 129:base + 130],
                                                   ALU.mult, ALU.add)
                            v.scalar_tensor_tensor(dcol[:, 2:3], dcol[:, 0:1],
                                                   tmr[:, cc:cc + 1],
                                                   xnt[:, base + 129:base + 130],
                                                   ALU.mult, ALU.add)
                            v.tensor_sub(dcol[:, 3:4], dcol[:, 1:2], kcol)
                            v.tensor_sub(dcol[:, 4:5], dcol[:, 2:3], rcol)
                            v.scalar_tensor_tensor(dcol[:, 5:6], dcol[:, 3:4],
                                                   selt[:, 0:1], kcol,
                                                   ALU.mult, ALU.add)
                            v.scalar_tensor_tensor(dcol[:, 6:7], dcol[:, 4:5],
                                                   selt[:, 0:1], rcol,
                                                   ALU.mult, ALU.add)
                            v.tensor_copy(kcol, dcol[:, 5:6])
                            v.tensor_copy(rcol, dcol[:, 6:7])
                        v.engine_nop().then_inc(s_dve, 1)
                        cnt["s_dve"] += 1

            @blk.gpsimd
            def _(g):
                for ti, (t0, L) in enumerate(TCOLS):
                    for cc in range(NCC):
                        g.wait_ge(s_dve, a_xn(ti, cc))
                        if cc >= 2:
                            g.wait_ge(s_dve, a_mix(ti, cc - 2))
                        base = cc * 513
                        g.tensor_tensor(d_t[cc % 2][:, 0:L], xnt[:, base:base + L],
                                        xnt[:, base + 1:base + 1 + L], ALU.subtract)
                        if ti == 0:
                            g.memset(d_t[cc % 2][:, 0:1], 0.0)
                        g.engine_nop().then_inc(s_pl, 1)
                        cnt["s_pl"] += 1

        assert cnt["s_pe"] == A_pe + NT * 17
        assert cnt["s_act"] == A_act + NT * 17
        assert cnt["s_dve"] == A_dve + NT * 34
        assert cnt["s_pl"] == A_pl + NT * 16

    if debug:
        xk_db = nc.dram_tensor("xk_db", [128, NCC * T], F16, kind="ExternalOutput").ap()
        xr_db = nc.dram_tensor("xr_db", [128, NCC * OWN], F16, kind="ExternalOutput").ap()
        with nc.Block("dbgA") as blk:
            @blk.sync
            def _(q):
                q.dma_start(out=xk_db, in_=xk[:]).then_inc(s_sp, 16)
                q.dma_start(out=xr_db, in_=xr[:]).then_inc(s_sp, 16)
                cnt["s_sp"] += 32

    if "B" not in phases:
        return fin(close_ab=True)

    with contextlib.ExitStack() as pes:
        sbp = lambda *a, **k: pes.enter_context(nc.sbuf_tensor(*a, **k))
        psp = lambda n, s: pes.enter_context(nc.psum_tensor(n, s, F32))
        wk_sl = [sbp("wk_slA", [128, C], F16), sbp("wk_slB", [128, C], F16)]
        wr_sl = [sbp("wr_slA", [128, C], F16), sbp("wr_slB", [128, C], F16)]
        onesT = sbp("onesT", [128, T], F16)
        wb_t = [sbp("wb_tA", [128, T], F32, align_bytes=128),
                sbp("wb_tB", [128, T], F32, align_bytes=128)]
        s_scan = sbp("s_scan", [128, T], F32, align_bytes=128)
        pad_b = sbp("pad_b", [128, 16], F32)
        krelu = [sbp("kreluA", [128, T], F16, align_bytes=128),
                 sbp("kreluB", [128, T], F16, align_bytes=128)]
        r_t = [sbp("r_tA", [128, OWN], F16), sbp("r_tB", [128, OWN], F16)]
        o_t = [sbp("o_tA", [128, OWN], F16), sbp("o_tB", [128, OWN], F16)]
        k_ps = [psp("k_psA", [128, 512]), psp("k_psB", [128, 512])]
        r_ps = [psp("r_psA", [128, 512]), psp("r_psB", [128, 512])]

        with nc.Block("phB") as blk:
            sp_a_end = cnt["s_sp"]
            w_base = cnt["s_w"]
            pe_base = cnt["s_pe"]
            act_base = cnt["s_act"]
            dve_base = cnt["s_dve"]
            sp_base = cnt["s_sp"]
            do_k = "k" in bparts
            do_r = "r" in bparts
            do_s = "s" in bparts and do_k
            do_o = "o" in bparts and do_s and do_r
            groups = ([("k", tc) for tc in TCOLS] if do_k else []) + \
                     ([("r", rc) for rc in RCOLS] if do_r else [])
            ngi = len(groups)
            dve_per_h = (1 if do_s else 0) + (1 if do_o else 0)
            w_per_h = (16 if do_k else 0) + (16 if do_r else 0)

            @blk.gpsimd
            def _(g):
                g.wait_ge(s_sp, sp_a_end)
                for h in range(NHH):
                    if h >= 2 and ngi:
                        g.wait_ge(s_pe, pe_base + (h - 1) * ngi)
                    if do_k:
                        g.dma_start(out=wk_sl[h % 2][:], in_=wks[h, :, :]).then_inc(s_w, 16)
                        cnt["s_w"] += 16
                    if do_r:
                        g.dma_start(out=wr_sl[h % 2][:], in_=wrs[h, :, :]).then_inc(s_w, 16)
                        cnt["s_w"] += 16

            @blk.tensor
            def _(t):
                for h in range(NHH):
                    t.wait_ge(s_w, w_base + w_per_h * (h + 1))
                    for gi, (kind, (t0, L)) in enumerate(groups):
                        pp = gi % 2
                        need = h * ngi + gi - 1
                        if need > 0:
                            t.wait_ge(s_act, act_base + need)
                        for cc in range(NCC):
                            if kind == "k":
                                mm = t.matmul(k_ps[pp][:, 0:L],
                                              wk_sl[h % 2][:, cc * 128:(cc + 1) * 128],
                                              xk[:, cc * T + t0: cc * T + t0 + L],
                                              start=(cc == 0), stop=(cc == NCC - 1))
                            else:
                                mm = t.matmul(r_ps[pp][:, 0:L],
                                              wr_sl[h % 2][:, cc * 128:(cc + 1) * 128],
                                              xr[:, cc * OWN + (t0 - WARM): cc * OWN + (t0 - WARM) + L],
                                              start=(cc == 0), stop=(cc == NCC - 1))
                        mm.then_inc(s_pe, 1)
                        cnt["s_pe"] += 1

            @blk.scalar
            def _(sc):
                for h in range(NHH):
                    if h >= 2 and dve_per_h:
                        sc.wait_ge(s_dve, dve_base + dve_per_h * (h - 1))
                    for gi, (kind, (t0, L)) in enumerate(groups):
                        pp = gi % 2
                        sc.wait_ge(s_pe, pe_base + h * ngi + gi + 1)
                        if kind == "k":
                            sc.activation(krelu[h % 2][:, t0:t0 + L], k_ps[pp][:, 0:L],
                                          AF.Relu).then_inc(s_act, 1)
                        else:
                            sc.activation(r_t[h % 2][:, t0 - WARM: t0 - WARM + L],
                                          r_ps[pp][:, 0:L],
                                          AF.Copy).then_inc(s_act, 1)
                        cnt["s_act"] += 1

            @blk.vector
            def _(v):
                if dve_per_h:
                    v.memset(onesT[:], 1.0)
                    v.memset(s_scan[:], 0.0)
                    v.memset(wb_t[1][:], 0.0)
                    v.tensor_scalar_mul(wb_t[0][:], onesT[:], wdec[:, 0:1])
                    v.memset(s_scan[:, 0:8], 0.0)
                    v.memset(s_scan[:, 8:16], 0.0)
                    for h in range(NHH):
                        if h >= 2 and do_o:
                            v.wait_ge(s_sp, sp_base + 16 * (h - 1))
                        v.wait_ge(s_act, act_base + h * ngi + ngi)
                        v.tensor_tensor_scan(s_scan[:], wb_t[h % 2][:], krelu[h % 2][:], 0.0,
                                             ALU.mult, ALU.add).then_inc(s_dve, 1)
                        cnt["s_dve"] += 1

                        if h + 1 < NHH:
                            v.tensor_scalar_mul(wb_t[(h + 1) % 2][:], onesT[:],
                                                wdec[:, h + 1:h + 2])
                        if do_o:
                            v.wait_ge(s_act, act_base + h * ngi + 9)
                            v.tensor_mul(o_t[h % 2][:], s_scan[:, WARM:T], r_t[h % 2][:])
                            v.memset(wb_t[h % 2][:, 0:1], 1.0).then_inc(s_dve, 1)
                            cnt["s_dve"] += 1

            @blk.sync
            def _(q):
                if do_o:
                    for h in range(NHH):
                        q.wait_ge(s_dve, dve_base + dve_per_h * (h + 1))
                        q.dma_start(out=o_sp[h * 128:(h + 1) * 128, :],
                                    in_=o_t[h % 2][:]).then_inc(s_sp, 16)
                        cnt["s_sp"] += 16

    ab_es.close()

    if "C" not in phases:
        return fin(close_ab=False)

    with contextlib.ExitStack() as pes:
        sbp = lambda *a, **k: pes.enter_context(nc.sbuf_tensor(*a, **k))
        psp = lambda n, s: pes.enter_context(nc.psum_tensor(n, s, F32))
        o_all = sbp("o_all", [128, NHH * OWN], F16)
        v_all = sbp("v_all", [128, NCC * OWN], F16)
        wv_sl = [sbp("wv_slA", [128, H], F16), sbp("wv_slB", [128, H], F16),
                 sbp("wv_slC", [128, H], F16), sbp("wv_slD", [128, H], F16)]
        v2_t = [sbp("v2_tA", [128, 512], F16), sbp("v2_tB", [128, 512], F16)]
        scrA = [sbp("scrA0", [128, 512], F16), sbp("scrA1", [128, 512], F16)]
        scrB = [sbp("scrB0", [128, 512], F16), sbp("scrB1", [128, 512], F16)]
        ab2_s = sbp("ab2_s", [128, 512], F16)
        bb2_s = sbp("bb2_s", [128, 512], F16)
        m_row = sbp("m_rowC", [1, 512], F32)
        m2_row = sbp("m2_rowC", [1, 512], F32)
        t_row = sbp("t_rowC", [1, 512], F32)
        v_rowp = [sbp("v_rowC0", [1, 512], F32), sbp("v_rowC1", [1, 512], F32)]
        sd_rowp = [sbp("sd_rowC0", [1, 512], F32), sbp("sd_rowC1", [1, 512], F32)]
        a_rowp = [sbp("a_rowC0", [1, 512], F32), sbp("a_rowC1", [1, 512], F32)]
        b_rowp = [sbp("b_rowC0", [1, 512], F32), sbp("b_rowC1", [1, 512], F32)]
        v_ps = [psp("v_psA", [128, 512]), psp("v_psB", [128, 512])]
        s1_ps = [psp("s1_psA2", [1, 512]), psp("s1_psB2", [1, 512])]
        s2_ps = [psp("s2_psA2", [1, 512]), psp("s2_psB2", [1, 512])]
        ab_ps = psp("ab_psC", [128, 512])
        bb_ps = psp("bb_psC", [128, 512])

        sp_b_end = cnt["s_sp"]
        w_c_base = cnt["s_w"]
        pe_c_base = cnt["s_pe"]
        act_c_base = cnt["s_act"]
        dve_c_base = cnt["s_dve"]
        ld_o_base = cnt["s_ld"]

        def act_evac(p, co, t1):
            return act_c_base + p * 66 + (co * 2 + t1) * 2 + 1

        def act_square(p, co, t1):
            return act_c_base + p * 66 + (co * 2 + t1) * 2 + 2

        def act_sqrt(p, t1):
            return act_c_base + p * 66 + 64 + t1 + 1

        def pe_vgrp(p, co, t1):
            return pe_c_base + p * 66 + co * 4 + t1 + 1

        def pe_stats(p, co, t1):
            return pe_c_base + p * 66 + co * 4 + 2 + t1 + 1

        def pe_bcast(p, t1):
            return pe_c_base + p * 66 + 64 + t1 + 1

        def dve_var(p, t1):
            return dve_c_base + p * (2 + 2 * 18) + t1 + 1

        def dve_rowsfinal(p, t1):
            return dve_c_base + p * 38 + 2 + t1 * 18 + 1

        def dve_copies(p, t1):
            return dve_c_base + p * 38 + 2 + t1 * 18 + 2

        def dve_apply(p, t1, cc):
            return dve_c_base + p * 38 + 2 + t1 * 18 + 2 + cc + 1

        with nc.Block("phC") as blk:
            @blk.sync
            def _(q):
                q.wait_ge(s_sp, sp_b_end)
                for h in range(0, NHH, 2):
                    q.dma_start(out=o_all[:, h * OWN:(h + 1) * OWN],
                                in_=o_sp[h * 128:(h + 1) * 128, :]).then_inc(s_ld, 16)
                    cnt["s_ld"] += 16
                for p in range(2):
                    for t1 in range(2):
                        tj = 2 * p + t1
                        for cc in range(NCC):
                            q.wait_ge(s_dve, dve_apply(p, t1, cc))
                            sl = slice(cc * OWN + tj * 512, cc * OWN + tj * 512 + 512)
                            q.dma_start(out=at_sp[cc * 128:(cc + 1) * 128,
                                                  tj * 512:tj * 512 + 512],
                                        in_=v_all[:, sl]).then_inc(s_sp, 16)
                            cnt["s_sp"] += 16

            @blk.gpsimd
            def _(g):
                g.wait_ge(s_sp, sp_b_end)
                for idx in range(2 * NCC):
                    p, co = idx // NCC, idx % NCC
                    if idx >= 4:
                        g.wait_ge(s_pe, pe_vgrp((idx - 3) // NCC, (idx - 3) % NCC, 1))
                    g.dma_start(out=wv_sl[idx % 4][:],
                                in_=wvs[co, :, :]).then_inc(s_w, 16)
                    cnt["s_w"] += 16

            @blk.tensor
            def _(t):
                for p in range(2):
                    if p == 1:
                        t.wait_ge(s_dve, dve_var(0, 1))
                    for co in range(NCC):
                        idx = p * NCC + co
                        t.wait_ge(s_w, w_c_base + 16 * (idx + 1))
                        for t1 in range(2):
                            tj = 2 * p + t1
                            grp = idx * 2 + t1
                            pp = grp % 2
                            if grp >= 2:
                                gp = grp - 2
                                t.wait_ge(s_act, act_evac(gp // 32, (gp % 32) // 2, gp % 2))
                            if idx == 0 and t1 == 0:
                                t.wait_ge(s_ld, ld_o_base + 16 * NHH)
                            for hh in range(NHH):
                                mm = t.matmul(v_ps[pp][:, 0:512],
                                              wv_sl[idx % 4][:, hh * 128:(hh + 1) * 128],
                                              o_all[:, hh * OWN + tj * 512: hh * OWN + tj * 512 + 512],
                                              start=(hh == 0), stop=(hh == NHH - 1))
                            mm.then_inc(s_pe, 1)
                            cnt["s_pe"] += 1
                        for t1 in range(2):
                            t.wait_ge(s_act, act_square(p, co, t1))
                            tj = 2 * p + t1
                            t.matmul(s1_ps[t1][0:1, :], ones_col16[:],
                                     v_all[:, co * OWN + tj * 512: co * OWN + tj * 512 + 512],
                                     start=(co == 0), stop=(co == NCC - 1))
                            mm = t.matmul(s2_ps[t1][0:1, :], ones_col16[:],
                                          v2_t[t1][:],
                                          start=(co == 0), stop=(co == NCC - 1))
                            mm.then_inc(s_pe, 1)
                            cnt["s_pe"] += 1
                    for t1 in range(2):
                        t.wait_ge(s_dve, dve_rowsfinal(p, t1))
                        if t1 == 1:
                            t.wait_ge(s_dve, dve_copies(p, 0))
                        elif p == 1:
                            t.wait_ge(s_dve, dve_copies(0, 1))
                        t.matmul(ab_ps[:], ones_row[:], a_rowp[t1][0:1, :],
                                 start=True, stop=True)
                        mm = t.matmul(bb_ps[:], ones_row[:], b_rowp[t1][0:1, :],
                                      start=True, stop=True)
                        mm.then_inc(s_pe, 1)
                        cnt["s_pe"] += 1

            @blk.scalar
            def _(sc):
                sc.wait_ge(s_sp, sp_b_end)
                for h in range(1, NHH, 2):
                    sc.dma_start(out=o_all[:, h * OWN:(h + 1) * OWN],
                                 in_=o_sp[h * 128:(h + 1) * 128, :]).then_inc(s_ld, 16)
                    cnt["s_ld"] += 16
                for p in range(2):
                    for co in range(NCC):
                        for t1 in range(2):
                            tj = 2 * p + t1
                            grp = (p * NCC + co) * 2 + t1
                            pp = grp % 2
                            sc.wait_ge(s_pe, pe_vgrp(p, co, t1))
                            if co >= 1:
                                sc.wait_ge(s_pe, pe_stats(p, co - 1, t1))
                            sl = slice(co * OWN + tj * 512, co * OWN + tj * 512 + 512)
                            sc.activation(v_all[:, sl], v_ps[pp][:, 0:512],
                                          AF.Copy).then_inc(s_act, 1)
                            cnt["s_act"] += 1
                            sc.activation(v2_t[t1][:], v_all[:, sl],
                                          AF.Square).then_inc(s_act, 1)
                            cnt["s_act"] += 1
                    for t1 in range(2):
                        sc.wait_ge(s_dve, dve_var(p, t1))
                        sc.activation(sd_rowp[t1][0:1, :], v_rowp[t1][0:1, :], AF.Sqrt,
                                      bias=epst[0:1, 0:1]).then_inc(s_act, 1)
                        cnt["s_act"] += 1

            @blk.vector
            def _(v):
                for p in range(2):
                    for t1 in range(2):
                        v.wait_ge(s_pe, pe_stats(p, NCC - 1, t1))
                        v.tensor_scalar_mul(m_row[0:1, :], s1_ps[t1][0:1, :], 1.0 / C)
                        v.tensor_scalar_mul(m2_row[0:1, :], s2_ps[t1][0:1, :], 1.0 / C)
                        v.tensor_mul(t_row[0:1, :], m_row[0:1, :], m_row[0:1, :])
                        v.tensor_sub(v_rowp[t1][0:1, :], m2_row[0:1, :],
                                     t_row[0:1, :]).then_inc(s_dve, 1)
                        cnt["s_dve"] += 1
                        v.tensor_scalar_mul(b_rowp[t1][0:1, :], m_row[0:1, :], -1.0)
                    for t1 in range(2):
                        tj = 2 * p + t1
                        v.wait_ge(s_act, act_sqrt(p, t1))
                        v.reciprocal(a_rowp[t1][0:1, :], sd_rowp[t1][0:1, :])
                        v.tensor_copy(t_row[0:1, :],
                                      a_rowp[t1][0:1, :]).then_inc(s_dve, 1)
                        cnt["s_dve"] += 1
                        v.wait_ge(s_pe, pe_bcast(p, t1))
                        v.tensor_copy(ab2_s[:], ab_ps[:])
                        v.tensor_copy(bb2_s[:], bb_ps[:]).then_inc(s_dve, 1)
                        cnt["s_dve"] += 1
                        def _sl(cc):
                            return slice(cc * OWN + tj * 512, cc * OWN + tj * 512 + 512)
                        for cc in range(NCC + 2):
                            if cc < NCC:
                                v.tensor_add(scrA[cc % 2][:], v_all[:, _sl(cc)], bb2_s[:])
                            if 1 <= cc <= NCC:
                                v.tensor_mul(scrB[(cc - 1) % 2][:],
                                             scrA[(cc - 1) % 2][:], ab2_s[:])
                            if cc >= 2:
                                v.tensor_scalar(v_all[:, _sl(cc - 2)],
                                                scrB[(cc - 2) % 2][:],
                                                gam[:, cc - 2:cc - 1],
                                                bet[:, cc - 2:cc - 1], ALU.mult,
                                                ALU.add).then_inc(s_dve, 1)
                                cnt["s_dve"] += 1
                        v.memset(scrA[0][:, 0:1], 0.0)

        assert cnt["s_pe"] == pe_c_base + 2 * 66, (cnt["s_pe"], pe_c_base)
        assert cnt["s_act"] == act_c_base + 2 * 66
        assert cnt["s_dve"] == dve_c_base + 2 * 38

    if "D" not in phases:
        return fin(close_ab=False)

    with contextlib.ExitStack() as pes:
        sbp = lambda *a, **k: pes.enter_context(nc.sbuf_tensor(*a, **k))
        psp = lambda n, s: pes.enter_context(nc.psum_tensor(n, s, F32))
        g_all = sbp("g_all", [128, NNB * 1024], F16)
        xn_tt = sbp("xn_tt", [128, NCC * 1024], F16)
        w0_sl = [sbp("w0_slA", [128, C], F16), sbp("w0_slB", [128, C], F16)]
        w1_sl = [sbp("w1_slA", [128, 2048], F16), sbp("w1_slB", [128, 2048], F16)]
        xc_t = [sbp("xc_tA", [128, 1024], F32), sbp("xc_tB", [128, 1024], F32)]
        at_t = [sbp("at_tA", [128, 1024], F16), sbp("at_tB", [128, 1024], F16)]
        ffn_t = [sbp("ffn_tA", [128, 1024], F16), sbp("ffn_tB", [128, 1024], F16)]
        ou_t = sbp("ou_t", [128, 1024], F32)
        d_ps = [psp("d_psA", [128, 1024]), psp("d_psB", [128, 1024])]
        e_ps = [psp("e_psA", [128, 1024]), psp("e_psB", [128, 1024])]

        with nc.Block("phD") as blk:
          d_prev = {}
          for di, (j0d, Lt) in enumerate(OCOLS2):
                sp_prev = cnt["s_sp"]
                w_base = cnt["s_w"]
                pe_base = cnt["s_pe"]
                act_base = cnt["s_act"]
                res_base = cnt["s_dve"]
                pe_d2_base = pe_base + NNB

                ld_xn_base = cnt["s_ld"]

                @blk.sync
                def _(q, j0d=j0d):
                    q.wait_ge(s_sp, sp_prev)
                    for cc in range(NCC):
                        q.dma_start(out=xn_tt[:, cc * 1024:(cc + 1) * 1024],
                                    in_=xn_sp[cc * 128:(cc + 1) * 128, j0d:j0d + 1024]).then_inc(s_ld, 16)
                        cnt["s_ld"] += 16

                @blk.gpsimd
                def _(g):
                    g.wait_ge(s_sp, sp_prev)
                    for nb in range(NNB):
                        if nb >= 2:
                            g.wait_ge(s_pe, pe_base + nb - 1)
                        g.dma_start(out=w0_sl[nb % 2][:], in_=w0s[nb, :, :]).then_inc(s_w, 16)
                        cnt["s_w"] += 16
                    for idx in range(4 * NCC):
                        co, qt = idx // 4, idx % 4
                        if idx >= 2:
                            g.wait_ge(s_pe, pe_d2_base + idx - 1)
                        g.dma_start(out=w1_sl[idx % 2][:],
                                    in_=w1s[co, :, qt * 2048:(qt + 1) * 2048]).then_inc(s_w, 16)
                        cnt["s_w"] += 16

                @blk.tensor
                def _(t, di=di):
                    for nb in range(NNB):
                        t.wait_ge(s_w, w_base + 16 * (nb + 1))
                        pp = nb % 2
                        if nb >= 2:
                            t.wait_ge(s_act, act_base + nb - 1)
                        elif di == 1:
                            t.wait_ge(s_act, act_base - NCC)
                        if nb == 0:
                            t.wait_ge(s_ld, ld_xn_base + 16 * NCC)
                        for cc in range(NCC):
                            t.matmul(d_ps[pp][:, 0:512],
                                     w0_sl[pp][:, cc * 128:(cc + 1) * 128],
                                     xn_tt[:, cc * 1024: cc * 1024 + 512],
                                     start=(cc == 0), stop=(cc == NCC - 1))
                            mm = t.matmul(d_ps[pp][:, 512:1024],
                                          w0_sl[pp][:, cc * 128:(cc + 1) * 128],
                                          xn_tt[:, cc * 1024 + 512: cc * 1024 + 1024],
                                          start=(cc == 0), stop=(cc == NCC - 1))
                        mm.then_inc(s_pe, 1)
                        cnt["s_pe"] += 1
                    t.wait_ge(s_act, act_base + NNB)
                    for co in range(NCC):
                        pp = co % 2
                        if co >= 1:
                            t.wait_ge(s_act, act_base + NNB + max(co - 1, 0))
                        elif di == 1:
                            t.wait_ge(s_act, act_base)
                        for qt in range(4):
                            t.wait_ge(s_w, w_base + 16 * (NNB + co * 4 + qt + 1))
                            for nn in range(16):
                                nb = qt * 16 + nn
                                t.matmul(e_ps[pp][:, 0:512],
                                         w1_sl[(co * 4 + qt) % 2][:, nn * 128:(nn + 1) * 128],
                                         g_all[:, nb * 1024: nb * 1024 + 512],
                                         start=(nb == 0), stop=(nb == NNB - 1))
                                mm = t.matmul(e_ps[pp][:, 512:1024],
                                              w1_sl[(co * 4 + qt) % 2][:, nn * 128:(nn + 1) * 128],
                                              g_all[:, nb * 1024 + 512: nb * 1024 + 1024],
                                              start=(nb == 0), stop=(nb == NNB - 1))
                            mm.then_inc(s_pe, 1)
                            cnt["s_pe"] += 1

                @blk.scalar
                def _(sc, di=di):
                    for nb in range(NNB):
                        sc.wait_ge(s_pe, pe_base + nb + 1)
                        if nb == 0 and di == 1:
                            sc.wait_ge(s_pe, pe_base)
                        sc.activation(g_all[:, nb * 1024:(nb + 1) * 1024],
                                      d_ps[nb % 2][:], AF.Gelu,
                                      bias=b0[:, nb:nb + 1]).then_inc(s_act, 1)
                        cnt["s_act"] += 1
                    for co in range(NCC):
                        sc.wait_ge(s_pe, pe_d2_base + 4 * (co + 1))
                        if co >= 2:
                            sc.wait_ge(s_dve, res_base + co - 1)
                        elif di == 1:
                            sc.wait_ge(s_dve, res_base)
                        sc.activation(ffn_t[co % 2][:], e_ps[co % 2][:], AF.Identity,
                                      bias=b1[:, co:co + 1]).then_inc(s_act, 1)
                        cnt["s_act"] += 1

                ld2_base = cnt["s_ld"]
                sp_out_base = cnt["s_sp"]

                @blk.sync
                def _(q, j0d=j0d):
                    for co in range(NCC):
                        if co >= 2:
                            q.wait_ge(s_dve, res_base + co - 1)
                            q.wait_ge(s_sp, sp_out_base + 16 * (co - 1))
                        q.dma_start(out=xc_t[co % 2][:],
                                    in_=x_fm[co * 128:(co + 1) * 128,
                                             WARM + j0d: WARM + j0d + 1024]).then_inc(s_ld, 16)
                        q.dma_start(out=at_t[co % 2][:],
                                    in_=at_sp[co * 128:(co + 1) * 128, j0d:j0d + 1024]).then_inc(s_ld, 16)
                        cnt["s_ld"] += 32
                        if co >= 1:
                            q.wait_ge(s_dve, res_base + co)
                            q.dma_start(out=out_fm[(co - 1) * 128:co * 128, j0d:j0d + 1024],
                                        in_=xc_t[(co - 1) % 2][:]).then_inc(s_sp, 16)
                            cnt["s_sp"] += 16
                    q.wait_ge(s_dve, res_base + NCC)
                    q.dma_start(out=out_fm[(NCC - 1) * 128:NCC * 128, j0d:j0d + 1024],
                                in_=xc_t[(NCC - 1) % 2][:]).then_inc(s_sp, 16)
                    cnt["s_sp"] += 16

                @blk.vector
                def _(v):
                    for co in range(NCC):
                        v.wait_ge(s_ld, ld2_base + 32 * (co + 1))
                        v.wait_ge(s_act, act_base + NNB + co + 1)
                        v.tensor_add(ou_t[:], xc_t[co % 2][:], at_t[co % 2][:])
                        v.tensor_add(xc_t[co % 2][:], ou_t[:],
                                     ffn_t[co % 2][:]).then_inc(s_dve, 1)
                        cnt["s_dve"] += 1

    return fin(close_ab=False)



def _host_layouts(inputs):
    f32 = np.float32
    Wk = np.ascontiguousarray(np.asarray(inputs["L0_attn_Wk"], f32))
    Wr = np.ascontiguousarray(np.asarray(inputs["L0_attn_Wr"], f32))
    Wv = np.ascontiguousarray(np.asarray(inputs["L0_attn_Wv"], f32))
    W0 = np.ascontiguousarray(np.asarray(inputs["L0_ffn_W0"], f32))
    W1 = np.ascontiguousarray(np.asarray(inputs["L0_ffn_W1"], f32))
    shared = {
        "WkS": np.ascontiguousarray(Wk.reshape(NHH, 128, NCC, 128).transpose(0, 3, 2, 1)).reshape(NHH, 128, C),
        "WrS": np.ascontiguousarray(Wr.reshape(NHH, 128, NCC, 128).transpose(0, 3, 2, 1)).reshape(NHH, 128, C),
        "WvS": np.ascontiguousarray(Wv.reshape(NCC, 128, NHH, 128).transpose(0, 3, 2, 1)).reshape(NCC, 128, H),
        "W0S": np.ascontiguousarray(W0.reshape(NNB, 128, NCC, 128).transpose(0, 3, 2, 1)).reshape(NNB, 128, C),
        "W1S": np.ascontiguousarray(W1.reshape(NCC, 128, NNB, 128).transpose(0, 3, 2, 1)).reshape(NCC, 128, NFF),
        "tmk": np.ascontiguousarray(np.asarray(inputs["L0_attn_time_mix_k"], f32).reshape(NCC, 128).T),
        "tmr": np.ascontiguousarray(np.asarray(inputs["L0_attn_time_mix_r"], f32).reshape(NCC, 128).T),
        "dec": np.ascontiguousarray(np.asarray(inputs["L0_attn_time_decay"], f32).reshape(NHH, 128).T),
        "gam": np.ascontiguousarray(np.asarray(inputs["L0_attn_ln_gamma"], f32).reshape(NCC, 128).T),
        "bet": np.ascontiguousarray(np.asarray(inputs["L0_attn_ln_beta"], f32).reshape(NCC, 128).T),
        "b0": np.ascontiguousarray(np.asarray(inputs["L0_ffn_bias0"], f32).reshape(NNB, 128).T),
        "b1": np.ascontiguousarray(np.asarray(inputs["L0_ffn_bias1"], f32).reshape(NCC, 128).T),
    }
    return shared


def make_in_maps(inputs):
    x = np.asarray(inputs["L0_attn_x"], np.float32)
    shared = _host_layouts(inputs)
    in_maps = []
    for c in range(8):
        b, half = c // 2, c % 2
        x_fm = np.zeros((C, T), np.float32)
        if half == 0:
            x_fm[:, WARM:] = x[b, 0:OWN].T
        else:
            x_fm[:, :] = x[b, OWN - WARM: 2 * OWN].T
        m = dict(shared)
        m["x_fm"] = np.ascontiguousarray(x_fm)
        m["sel"] = np.full((128, 1), 1.0 if half == 0 else 0.0, np.float32)
        in_maps.append(m)
    return in_maps


def kernel(**inputs):
    nc = build_program(debug=False)
    in_maps = make_in_maps(inputs)
    res = run_bass_kernel_spmd(nc, in_maps, core_ids=list(range(8)))
    out = np.empty((4, 4096, C), np.float32)
    for c, r in enumerate(res.results):
        b, half = c // 2, c % 2
        out[b, half * OWN:(half + 1) * OWN, :] = r["out_fm"].T
    return out
